# version 06 (final)
# baseline (speedup 1.0000x reference)
import numpy as np
import ml_dtypes
from contextlib import ExitStack
import concourse.bass as bass
import concourse.mybir as mybir
from concourse.bass_utils import run_bass_kernel_spmd

F32 = mybir.dt.float32
BF16 = mybir.dt.bfloat16
AF = mybir.ActivationFunctionType
ALU = mybir.AluOpType
AX = mybir.AxisListType

N_CORES = 8
TOK = 1024
NTT = TOK // 128
D = 2048
NIC = D // 128
FF = 8192
LN_EPS = 1e-5


def ffn_perm():
    j = np.arange(FF) // 512
    r = (np.arange(FF) % 512) // 128
    gg = np.arange(FF) % 128
    return 4 * (128 * j + gg) + r

def host_prep_l0ffn(inputs):
    perm = ffn_perm()
    wg = np.ascontiguousarray(np.asarray(inputs["L0_ffn_w_gate"])[:, perm])
    wu = np.ascontiguousarray(np.asarray(inputs["L0_ffn_w_up"])[:, perm])
    wd = np.ascontiguousarray(np.asarray(inputs["L0_ffn_w_down"])[perm, :])
    return wg, wu, wd

def ident_bf16():
    return np.eye(128, dtype=ml_dtypes.bfloat16)


class P:
    def __init__(self, nc):
        self.nc = nc
        self.es = ExitStack()
        self.sem = {}
        self.cnt = {}
        self.t = nc.tensor
        self.v = nc.vector
        self.s = nc.scalar
        self.g = nc.gpsimd
        self.y = nc.sync

    def sems(self, *names):
        for n in names:
            self.sem[n] = self.es.enter_context(self.nc.semaphore(n))
            self.cnt[n] = 0

    def sb(self, name, shape, dtype):
        return self.es.enter_context(self.nc.sbuf_tensor(name, shape, dtype))

    def ps(self, name, shape, dtype=F32):
        return self.es.enter_context(self.nc.psum_tensor(name, shape, dtype))

    def inc(self, inst, sname, n=1):
        inst.then_inc(self.sem[sname], n)
        self.cnt[sname] += n
        return self.cnt[sname]

    def dma(self, q, out, in_, sname):
        inst = q.dma_start(out=out, in_=in_)
        return self.inc(inst, sname, 16)

    def w(self, eng, sname, val=None):
        if val is None:
            val = self.cnt[sname]
        if val > 0:
            eng.wait_ge(self.sem[sname], val)
        return val

    def close(self):
        self.es.close()


def make_ln_tiles(p, scratch, name="", nbuf=2):
    return {
        "col": p.sb("lncol" + name, [128, NTT * 4], F32),
        "lnsq": scratch[:, 0:2048],
        "xnb": p.sb("xnb" + name, [128, nbuf * 2048], BF16),
    }


def emit_ln2(p, xs, xnT, ln, ptA, ptB, ident, x_ready, n_tt=NTT, n_ic=NIC, width=2048, nbuf=2):
    v, s, t = p.v, p.s, p.t
    col, lnsq, xnb = ln["col"], ln["lnsq"], ln["xnb"]
    inv = 1.0 / width
    iz = v.memset(col[:, 0:n_tt * 4], 0.0)
    z_c = p.inc(iz, "V")
    p.w(s, "V", z_c)
    tA_done = 0
    tB_done = 0
    evA = 0
    evB = 0
    apply_cnt = {}
    for tt in range(n_tt):
        c0 = 4 * tt
        xt = xs[:, tt * width:(tt + 1) * width]
        sem, cval = x_ready[tt]
        p.w(v, sem, cval)
        p.w(s, sem, cval)
        i1 = v.tensor_reduce(col[:, c0:c0 + 1], xt, AX.X, ALU.add)
        red_c = p.inc(i1, "V")
        i2 = s.activation(lnsq[:, 0:width], xt, AF.Square, accum_out=col[:, c0 + 1:c0 + 2])
        sq_c = p.inc(i2, "S")
        p.w(v, "V", red_c)
        i3 = v.tensor_scalar_mul(col[:, c0 + 2:c0 + 3], col[:, c0:c0 + 1], inv)
        mean_c = p.inc(i3, "V")
        p.w(v, "S", sq_c)
        p.w(v, "V", mean_c)
        i4 = v.tensor_tensor(col[:, c0 + 3:c0 + 4], col[:, c0 + 2:c0 + 3], col[:, c0 + 2:c0 + 3], ALU.mult)
        m2_c = p.inc(i4, "V")
        p.w(v, "V", m2_c)
        i5 = v.scalar_tensor_tensor(col[:, c0 + 3:c0 + 4], col[:, c0 + 1:c0 + 2], inv,
                                    col[:, c0 + 3:c0 + 4], ALU.mult, ALU.subtract)
        v5_c = p.inc(i5, "V")
        p.w(v, "V", v5_c)
        i5b = v.tensor_scalar_add(col[:, c0 + 3:c0 + 4], col[:, c0 + 3:c0 + 4], LN_EPS)
        var_c = p.inc(i5b, "V")
        p.w(s, "V", var_c)
        i6 = s.activation(col[:, c0 + 1:c0 + 2], col[:, c0 + 3:c0 + 4], AF.Sqrt)
        sqr_c = p.inc(i6, "S")
        p.w(v, "S", sqr_c)
        i7 = v.reciprocal(col[:, c0 + 3:c0 + 4], col[:, c0 + 1:c0 + 2])
        rec_c = p.inc(i7, "V")
        p.w(v, "V", rec_c)
        buf = tt % nbuf
        if tt >= nbuf:
            p.w(v, "T", apply_cnt[tt - nbuf])
        xnbv = xnb[:, buf * 2048:buf * 2048 + width]
        i8 = v.tensor_scalar(xnbv, xt, col[:, c0 + 2:c0 + 3], col[:, c0 + 3:c0 + 4],
                             ALU.subtract, ALU.mult)
        ap_c = p.inc(i8, "V")
        p.w(t, "V", ap_c)
        if tt >= 1:
            p.w(t, "V", evA)
        for ic in range(min(n_ic, 8)):
            ii = t.transpose(ptA[:, ic * 128:(ic + 1) * 128], xnbv[:, ic * 128:(ic + 1) * 128], ident)
        tA_done = p.inc(ii, "T")
        if n_ic > 8:
            if tt >= 1:
                p.w(t, "V", evB)
            for ic in range(8, n_ic):
                ii = t.transpose(ptB[:, (ic - 8) * 128:(ic - 7) * 128], xnbv[:, ic * 128:(ic + 1) * 128], ident)
            tB_done = p.inc(ii, "T")
        apply_cnt[tt] = p.cnt["T"]
        p.w(v, "T", tA_done)
        na = min(n_ic, 8)
        outA = xnT[:, tt * 128: ].rearrange("p (ic rest) -> p ic rest", rest=TOK)[:, 0:na, 0:128] \
            if False else None
        oA = xnT[:, :].rearrange("p (ic q) -> p ic q", q=TOK)[:, 0:na, tt * 128:tt * 128 + 128]
        iA = ptA[:, :].rearrange("p (ic q) -> p ic q", q=128)[:, 0:na, :]
        i9 = v.tensor_copy(oA, iA)
        evA = p.inc(i9, "V")
        if n_ic > 8:
            p.w(v, "T", tB_done)
            nb = n_ic - 8
            oB = xnT[:, :].rearrange("p (ic q) -> p ic q", q=TOK)[:, 8:8 + nb, tt * 128:tt * 128 + 128]
            iB = ptB[:, :].rearrange("p (ic q) -> p ic q", q=128)[:, 0:nb, :]
            i10 = v.tensor_copy(oB, iB)
            evB = p.inc(i10, "V")
    return ("V", p.cnt["V"])


def emit_l0ffn(p, xs, xnT, fb, xnT_ready, coef=0.5):
    v, s, t, g = p.v, p.s, p.t, p.g
    wgub = fb["wgub"]
    wdb = fb["wdb"]
    hf = fb["hf"]
    habs = fb["habs"]
    hsb = fb["hsb"]
    tsc = fb["tsc"]
    pg, pu = fb["pg"], fb["pu"]
    pd = fb["pd"]
    wg_d, wu_d, wd_d = fb["wg_d"], fb["wu_d"], fb["wd_d"]

    gu_buf_free = [0, 0]
    wd_buf_free = [0, 0]
    gu_loaded = [0, 0]
    wd_loaded = [0, 0]
    pgpu_free = [0, 0]
    pgpu_free_S = [0, 0]
    pd_free = [0, 0]
    hs_ready = 0
    down_T = {}
    sem_x, cnt_x = xnT_ready

    def load_gu(j, c, b):
        col0 = 128 * (4 * j + c)
        p.w(g, "T", gu_buf_free[b])
        out = wgub[:, b * 4096:b * 4096 + 2048].rearrange("p (ic o) -> p ic o", o=128)
        in_ = wg_d[:, col0:col0 + 128].rearrange("(ic q) o -> q ic o", q=128)
        p.dma(g, out, in_, "gd")
        out = wgub[:, b * 4096 + 2048:b * 4096 + 4096].rearrange("p (ic o) -> p ic o", o=128)
        in_ = wu_d[:, col0:col0 + 128].rearrange("(ic q) o -> q ic o", q=128)
        gu_loaded[b] = p.dma(g, out, in_, "gd")

    def load_wd(j, b):
        p.w(g, "V", wd_buf_free[b])
        out = wdb[:, b * 8192:(b + 1) * 8192].rearrange("p (r h) -> p r h", h=2048)
        in_ = wd_d[512 * j:512 * (j + 1), :].rearrange("(r q) h -> q r h", q=128)
        wd_loaded[b] = p.dma(g, out, in_, "gd")

    load_gu(0, 0, 0)
    load_gu(0, 1, 1)
    load_wd(0, 0)

    nextload = [(j, c) for j in range(16) for c in range(4)][2:]
    li = 0

    hs_guard = 0
    for j in range(16):
        p.w(t, sem_x, cnt_x)
        p.w(p.s, "V", hs_guard)
        for c in range(4):
            b = (4 * j + c) % 2
            p.w(t, "gd", gu_loaded[b])
            for th in range(2):
                par = th
                p.w(t, "S", pgpu_free_S[par])
                p.w(t, "V", pgpu_free[par])
                pgv = pg[:, par * 512:(par + 1) * 512]
                puv = pu[:, par * 512:(par + 1) * 512]
                for ic in range(NIC):
                    lg = wgub[:, b * 4096 + ic * 128: b * 4096 + (ic + 1) * 128]
                    lu = wgub[:, b * 4096 + 2048 + ic * 128: b * 4096 + 2048 + (ic + 1) * 128]
                    rx = xnT[:, ic * TOK + th * 512: ic * TOK + th * 512 + 512]
                    t.matmul(pgv, lg, rx, start=(ic == 0), stop=(ic == NIC - 1))
                    ii = t.matmul(puv, lu, rx, start=(ic == 0), stop=(ic == NIC - 1))
                mm_c = p.inc(ii, "T")
                if th == 1:
                    gu_buf_free[b] = mm_c
                    if li < len(nextload):
                        nj, nch = nextload[li]
                        li += 1
                        load_gu(nj, nch, b)
                hv = hf[:, c * 1024 + th * 512: c * 1024 + th * 512 + 512]
                p.w(s, "T", mm_c)
                i1 = s.activation(hv, pgv, AF.Silu)
                sil_c = p.inc(i1, "S")
                pgpu_free_S[par] = sil_c
                p.w(v, "S", sil_c)
                i2 = v.tensor_tensor(hv, hv, puv, ALU.mult)
                mul_c = p.inc(i2, "V")
                pgpu_free[par] = mul_c
        gu_done = p.cnt["V"]

        A = [habs[:, r * 512:(r + 1) * 512] for r in range(4)]
        H = [hf[:, r * 1024:(r + 1) * 1024] for r in range(4)]
        HS = [hsb[:, r * 1024:(r + 1) * 1024] for r in range(4)]
        if j >= 1:
            p.w(v, "T", down_T[j - 1])
        for hh in range(2):
            sl = slice(hh * 512, hh * 512 + 512)
            p.w(s, "V", gu_done)
            if hh == 1:
                p.w(s, "V", p.cnt["V"])
            for r in range(4):
                ia = s.activation(A[r], H[r][:, sl], AF.Abs)
            abs_c = p.inc(ia, "S")
            p.w(v, "S", abs_c)
            tp = tsc[:, 0 * 512:1 * 512]
            tq = tsc[:, 1 * 512:2 * 512]
            tr = tsc[:, 2 * 512:3 * 512]
            tss = tsc[:, 3 * 512:4 * 512]
            tz = tsc[:, 4 * 512:5 * 512]
            tth = tsc[:, 5 * 512:6 * 512]
            v.tensor_tensor(tp, A[0], A[1], ALU.max)
            v.tensor_tensor(tq, A[0], A[1], ALU.min)
            v.tensor_tensor(tr, A[2], A[3], ALU.max)
            ii = v.tensor_tensor(tss, A[2], A[3], ALU.min)
            cN = p.inc(ii, "V"); p.w(v, "V", cN)
            v.tensor_tensor(tz, tp, tr, ALU.is_ge)
            ii = v.tensor_tensor(tq, tq, tss, ALU.subtract)
            cN = p.inc(ii, "V"); p.w(v, "V", cN)
            ii = v.tensor_tensor(tz, tz, tq, ALU.mult)
            cN = p.inc(ii, "V"); p.w(v, "V", cN)
            ii = v.tensor_tensor(tz, tz, tss, ALU.add)
            cN = p.inc(ii, "V"); p.w(v, "V", cN)
            ii = v.tensor_tensor(tth, tp, tr, ALU.min)
            cN = p.inc(ii, "V"); p.w(v, "V", cN)
            ii = v.tensor_tensor(tth, tth, tz, ALU.max)
            cN = p.inc(ii, "V"); p.w(v, "V", cN)
            tk = tsc[:, 4 * 512:5 * 512]
            tk2 = tsc[:, 3 * 512:4 * 512]
            for r in range(4):
                ka = tk if r % 2 == 0 else tk2
                ii = v.tensor_tensor(ka, A[r], tth, ALU.is_ge)
                cN = p.inc(ii, "V"); p.w(v, "V", cN)
                ii = v.tensor_tensor(HS[r][:, sl], ka, H[r][:, sl], ALU.mult)
        hs_ready = p.inc(ii, "V")
        hs_guard = hs_ready

        b = j % 2
        p.w(t, "gd", wd_loaded[b])
        p.w(t, "V", hs_ready)
        for tt in range(NTT):
            for sl4 in range(4):
                par = (4 * tt + sl4) % 2
                p.w(t, "V", pd_free[par])
                pdv = pd[:, par * 512:(par + 1) * 512]
                for c in range(4):
                    lh = hsb[:, c * 1024 + tt * 128: c * 1024 + (tt + 1) * 128]
                    rw = wdb[:, b * 8192 + c * 2048 + sl4 * 512: b * 8192 + c * 2048 + (sl4 + 1) * 512]
                    ii = t.matmul(pdv, lh, rw, start=(c == 0), stop=(c == 3))
                dmm = p.inc(ii, "T")
                xv = xs[:, tt * 2048 + sl4 * 512: tt * 2048 + (sl4 + 1) * 512]
                p.w(v, "T", dmm)
                ie = v.scalar_tensor_tensor(xv, pdv, coef, xv, ALU.mult, ALU.add)
                pd_free[par] = p.inc(ie, "V")
        down_T[j] = p.cnt["T"]
        wd_buf_free[b] = p.cnt["V"]
        if j + 1 < 16:
            load_wd(j + 1, (j + 1) % 2)

    return ("V", p.cnt["V"])


def make_l0ffn_tiles(p, scratch, wg_d, wu_d, wd_d):
    return {
        "wgub": p.sb("wgub", [128, 2 * 4096], BF16),
        "wdb": p.sb("wdb", [128, 2 * 8192], BF16),
        "hf": p.sb("hf", [128, 4 * 1024], F32),
        "habs": p.sb("habs", [128, 4 * 512], F32),
        "hsb": p.sb("hsb", [128, 4 * 1024], BF16),
        "tsc": scratch[:, 0:3072],
        "pg": p.ps("pg", [128, 2 * 512], F32),
        "pu": p.ps("pu", [128, 2 * 512], F32),
        "pd": p.ps("pd", [128, 2 * 512], F32),
        "wg_d": wg_d, "wu_d": wu_d, "wd_d": wd_d,
    }


def build_l0ffn_launch(coef=0.5):
    nc = bass.Bass("TRN2", target_bir_lowering=False)
    x_in = nc.dram_tensor("x_in", [TOK, D], F32, kind="ExternalInput").ap()
    wg_d = nc.dram_tensor("wg", [D, FF], F32, kind="ExternalInput").ap()
    wu_d = nc.dram_tensor("wu", [D, FF], F32, kind="ExternalInput").ap()
    wd_d = nc.dram_tensor("wd", [FF, D], F32, kind="ExternalInput").ap()
    ident_d = nc.dram_tensor("ident_in", [128, 128], BF16, kind="ExternalInput").ap()
    x_out = nc.dram_tensor("x_out", [TOK, D], F32, kind="ExternalOutput").ap()

    p = P(nc)
    p.sems("gd", "yd", "st", "T", "V", "S")
    xs = p.sb("xs", [128, NTT * 2048], F32)
    xnT = p.sb("xnT", [128, NIC * TOK], BF16)
    ident = p.sb("ident", [128, 128], BF16)
    scratch = p.sb("scratch", [128, 3072], F32)
    ln = make_ln_tiles(p, scratch)
    ptA = p.ps("ptA", [128, 1024], BF16)
    ptB = p.ps("ptB", [128, 1024], BF16)
    fb = make_l0ffn_tiles(p, scratch, wg_d, wu_d, wd_d)

    with nc.Block() as block:
        @block.sync
        def _(sy):
            p.dma(p.y, ident[:, :], ident_d, "yd")
            x_ready = []
            for tt in range(NTT):
                cval = p.dma(p.y, xs[:, tt * 2048:(tt + 1) * 2048],
                             x_in[tt * 128:(tt + 1) * 128, :], "yd")
                x_ready.append(("yd", cval))
            emit_ln2(p, xs, xnT, ln, ptA, ptB, ident[:, :], x_ready)
            xnT_ready = ("V", p.cnt["V"])
            emit_l0ffn(p, xs, xnT, fb, xnT_ready, coef=coef)
            done = p.cnt["V"]
            p.w(p.y, "V", done)
            for tt in range(NTT):
                p.dma(p.y, x_out[tt * 128:(tt + 1) * 128, :],
                      xs[:, tt * 2048:(tt + 1) * 2048], "st")
            p.w(p.y, "st")
    p.close()
    return nc


def run_l0ffn(x_full, wg, wu, wd):
    nc = build_l0ffn_launch()
    ident = ident_bf16()
    in_maps = []
    xf = np.ascontiguousarray(x_full.reshape(8192, D))
    for c in range(N_CORES):
        in_maps.append({
            "x_in": np.ascontiguousarray(xf[c * TOK:(c + 1) * TOK]),
            "wg": wg, "wu": wu, "wd": wd, "ident_in": ident,
        })
    res = run_bass_kernel_spmd(nc, in_maps, core_ids=list(range(N_CORES)))
    out = np.concatenate([np.asarray(r["x_out"]) for r in res.results], axis=0)
    if debug:
        return out.reshape(x_full.shape), res
    return out.reshape(x_full.shape)


ATTN_SCALE = float(128 ** -0.5)

def host_prep_l0attn(inputs, core):
    half = core % 2
    t0 = half * TOK
    pos = np.arange(t0, t0 + TOK, dtype=np.float64)
    inv = 1.0 / (10000.0 ** (np.arange(64, dtype=np.float64) / 64.0))
    ang = pos[:, None] * inv[None, :]
    cs = np.concatenate([np.cos(ang), np.sin(ang)], axis=1).astype(np.float32)
    tri = np.where(np.arange(128)[None, :] > np.arange(128)[:, None], -1e30, 0.0).astype(np.float32)
    z = np.zeros_like(tri)
    tri2 = np.stack([tri, z] if half == 0 else [z, tri])
    m1 = np.full((128, 1024), -1e30, np.float32) if half == 0 else np.zeros((128, 1024), np.float32)
    return cs, tri2, m1


def emit_dequant(p, db, w_dram_list, wdq_in_d, core_id_unused=None):
    v, s, g, y = p.v, p.s, p.g, p.y
    dqw = db["dqw"]
    dpos = db["dpos"]
    dneg = db["dneg"]
    dsm = db["dsm"]
    dqo = db["dqo"]
    dq_lastread = [0, 0]
    for w in range(4):
        for ch in range(2):
            buf = (2 * w + ch) % 2
            dq = dqw[:, buf * 2048:(buf + 1) * 2048]
            p.w(p.y, "V", dq_lastread[buf])
            ldc = p.dma(p.y, dq, w_dram_list[w][ch * 128:(ch + 1) * 128, :], "yd")
            p.w(v, "yd", ldc)
            c0 = 16 * (0)
            s16 = dsm[:, 0:16]; clipm = dsm[:, 16:32]; thr = dsm[:, 32:48]; nthr = dsm[:, 48:64]
            ii = v.tensor_reduce(s16, dq.rearrange("p (g j) -> p g j", j=128), AX.X, ALU.add,
                                 apply_absolute_value=True)
            cN = p.inc(ii, "V"); p.w(v, "V", cN)
            ii = v.tensor_scalar(clipm, s16, 1.0 / 128, 1e-8, ALU.mult, ALU.max)
            cN = p.inc(ii, "V"); p.w(v, "V", cN)
            ii = v.tensor_scalar_mul(thr, clipm, 0.5)
            cN = p.inc(ii, "V"); p.w(v, "V", cN)
            ii = v.tensor_scalar_mul(nthr, thr, -1.0)
            cN = p.inc(ii, "V"); p.w(v, "V", cN)
            dq3 = dq.rearrange("p (g j) -> p g j", j=128)
            ii = v.tensor_tensor(dpos.rearrange("p (g j) -> p g j", j=128), dq3,
                                 thr.unsqueeze(2).broadcast_to((128, 16, 128)), ALU.is_gt)
            p.inc(ii, "V")
            ii = v.tensor_tensor(dneg.rearrange("p (g j) -> p g j", j=128), dq3,
                                 nthr.unsqueeze(2).broadcast_to((128, 16, 128)), ALU.is_lt)
            cN = p.inc(ii, "V"); p.w(v, "V", cN)
            dq_lastread[buf] = cN
            ii = v.tensor_tensor(dpos, dpos, dneg, ALU.subtract)
            cN = p.inc(ii, "V"); p.w(v, "V", cN)
            ii = v.tensor_tensor(dqo.rearrange("p (g j) -> p g j", j=128),
                                 dpos.rearrange("p (g j) -> p g j", j=128),
                                 clipm.unsqueeze(2).broadcast_to((128, 16, 128)), ALU.mult)
            cN = p.inc(ii, "V"); p.w(y, "V", cN)
            stc = p.dma(p.y, wdq_in_d[w * 256 + ch * 128: w * 256 + (ch + 1) * 128, :], dqo, "yd")
            p.w(v, "yd", stc)
    return p.cnt["yd"]


def emit_rope(p, src_f32, dst_bf, cs, tt, scr2, nheads=4):
    v = p.v
    s3 = src_f32.rearrange("p (h j) -> p h j", j=128)
    d3 = dst_bf.rearrange("p (h j) -> p h j", j=128)
    x1 = s3[:, :, 0:64]
    x2 = s3[:, :, 64:128]
    cosb = cs[:, tt * 128:tt * 128 + 64].unsqueeze(1).broadcast_to((128, nheads, 64))
    sinb = cs[:, tt * 128 + 64:tt * 128 + 128].unsqueeze(1).broadcast_to((128, nheads, 64))
    u = scr2[:, 0:nheads * 64].rearrange("p (h j) -> p h j", j=64)
    w2 = scr2[:, nheads * 64:2 * nheads * 64].rearrange("p (h j) -> p h j", j=64)
    v.tensor_tensor(u, x1, cosb, ALU.mult)
    ii = v.tensor_tensor(w2, x2, sinb, ALU.mult)
    cN = p.inc(ii, "V"); p.w(v, "V", cN)
    ii = v.tensor_tensor(d3[:, :, 0:64], u, w2, ALU.subtract)
    p.inc(ii, "V")
    v.tensor_tensor(u, x1, sinb, ALU.mult)
    ii = v.tensor_tensor(w2, x2, cosb, ALU.mult)
    cN = p.inc(ii, "V"); p.w(v, "V", cN)
    ii = v.tensor_tensor(d3[:, :, 64:128], u, w2, ALU.add)
    return p.inc(ii, "V")


def emit_l0attn(p, xs, xnT, ab, xnT_ready, ln_pt_done, dq_done, dbg=None, cc_wdq=1):
    v, s, t, g, y = p.v, p.s, p.t, p.g, p.y
    qct = ab["qct"]; wsl = ab["wsl"]; kTt = ab["kTt"]; vbt = ab["vbt"]
    qro = ab["qro"]; cs = ab["cs"]; tri = ab["tri"]; m1 = ab["m1"]
    cols = ab["cols"]; ones1 = ab["ones1"]; ident = ab["ident"]
    scratch = ab["scratch"]; pp1 = ab["pp1"]; ppo = ab["ppo"]
    wdq_out_d = ab["wdq_out_d"]; kv_in_d = ab["kv_in_d"]; kv_out_d = ab["kv_out_d"]
    qtmp = scratch[:, 0:1024]
    scr2 = scratch[:, 2048:2560]
    P_un = scratch[:, 1024:2048].bitcast(BF16)
    pt = ppo[:, 1024:1536].bitcast(BF16)
    pctx = ppo[:, 1536:2048]
    pq = [ppo[:, 0:512], ppo[:, 512:1024]]

    im = v.memset(ones1, 1.0)
    p.inc(im, "V")
    qro_guard = {0: ("V", 0), 1: ("V", 0)}
    pt_evac = [0]

    def load_slab(w, oslab, buf, extra_wait=0):
        p.w(y, "T", slab_free[buf])
        p.w(y, "V", max(dq_done, extra_wait))
        for ic in range(NIC):
            for piece in range(2):
                r = 2 * oslab + piece
                rows = wdq_out_d[r * 1024 + w * 256: r * 1024 + w * 256 + 256,
                                 ic * 128:(ic + 1) * 128]
                out = wsl[:, buf * 8192 + ic * 512 + piece * 256:
                          buf * 8192 + ic * 512 + piece * 256 + 256]
                ii = y.dma_start_transpose(out, rows)
                p.inc(ii, "yd", 16)
        slab_loaded[buf] = p.cnt["yd"]

    slab_free = [0, 0]
    slab_loaded = [0, 0]
    p.w(y, "cc", 1)
    load_slab(0, 0, 0)
    load_slab(0, 1, 1)
    slabs = [(w, o) for w in range(3) for o in range(4)]
    li = 2
    pq_free = [ln_pt_done, ln_pt_done]
    rope_done = 0
    kv_stores = 0
    sem_x, cnt_x = xnT_ready
    for si, (w, oslab) in enumerate(slabs):
        buf = si % 2
        p.w(t, "yd", slab_loaded[buf])
        p.w(t, sem_x, cnt_x)
        for tt in range(NTT):
            par = tt % 2
            p.w(t, "V", pq_free[par])
            psl = pq[par]
            for ic in range(NIC):
                ii = t.matmul(psl, xnT[:, ic * TOK + tt * 128: ic * TOK + (tt + 1) * 128],
                              wsl[:, buf * 8192 + ic * 512: buf * 8192 + (ic + 1) * 512],
                              start=(ic == 0), stop=(ic == NIC - 1))
            mm_c = p.inc(ii, "T")
            if tt == NTT - 1:
                slab_free[buf] = mm_c
                if li < len(slabs):
                    load_slab(slabs[li][0], slabs[li][1], buf)
                    li += 1
            p.w(v, "T", mm_c)
            gsem, gcnt = qro_guard[tt % 2]
            if w == 2:
                vv = qro[:, (tt % 2) * 512:(tt % 2) * 512 + 512]
                p.w(v, gsem, gcnt)
                ii = v.tensor_copy(vv, psl)
                ev_c = p.inc(ii, "V")
                pq_free[par] = ev_c
                p.w(y, "V", ev_c)
                kv_stores = p.dma(y, kv_in_d[tt * 128:(tt + 1) * 128,
                                             2048 + oslab * 512: 2048 + (oslab + 1) * 512], vv, "yd")
                qro_guard[(tt % 2)] = ("yd", kv_stores)
            else:
                qh = qtmp[:, par * 512:par * 512 + 512]
                ii = v.tensor_copy(qh, psl)
                ev_c = p.inc(ii, "V"); p.w(v, "V", ev_c)
                pq_free[par] = ev_c
                dst = qro[:, (tt % 2) * 512:(tt % 2) * 512 + 512]
                p.w(v, gsem, gcnt)
                rope_done = emit_rope(p, qh, dst, cs, tt, scr2)
                if w == 0:
                    p.w(t, "V", rope_done)
                    p.w(t, "V", pt_evac[0])
                    for h in range(4):
                        hd = oslab * 4 + h
                        ii = t.transpose(pt[:, h * 128:(h + 1) * 128],
                                         dst[:, h * 128:(h + 1) * 128], ident)
                    tr_c = p.inc(ii, "T")
                    p.w(v, "T", tr_c)
                    oV = qct[:, :].rearrange("p (hd q) -> p hd q", q=TOK)[:, oslab * 4:oslab * 4 + 4,
                                                                         tt * 128:tt * 128 + 128]
                    iV = pt[:, :].rearrange("p (h q) -> p h q", q=128)[:, 0:4, :]
                    ii = v.tensor_copy(oV, iV)
                    pt_evac[0] = p.inc(ii, "V")
                    qro_guard[(tt % 2)] = ("V", pt_evac[0])
                else:
                    p.w(y, "V", rope_done)
                    kv_stores = p.dma(y, kv_in_d[tt * 128:(tt + 1) * 128,
                                                 oslab * 512:(oslab + 1) * 512], dst, "yd")
                    qro_guard[(tt % 2)] = ("yd", kv_stores)
    p.w(g, "yd", p.cnt["yd"])
    icc = g.collective_compute("AllGather", ALU.bypass,
                               replica_groups=[[0, 1], [2, 3], [4, 5], [6, 7]],
                               ins=[kv_in_d[:, :].opt()], outs=[kv_out_d[:, :].opt()])
    p.inc(icc, "cc")
    icc2 = g.collective_compute("AllGather", ALU.bypass,
                                replica_groups=[[0, 1], [2, 3], [4, 5], [6, 7]],
                                ins=[ab["bar_in"][:, :].opt()], outs=[ab["bar_out2"][:, :].opt()])
    cc_kv = p.inc(icc2, "cc")
    if dbg is not None and dbg.get("p2dump"):
        p.w(y, "cc", cc_kv)
        p.dma(y, dbg["wdq"], ab["wdq_out_d"][:, :], "st")
        p.dma(y, dbg["kv"], kv_out_d[:, :], "st")
        p.w(y, "V", p.cnt["V"])
        p.dma(y, dbg["qct"].rearrange("(hd q2) q -> q2 hd q", q2=128),
              qct[:, :].rearrange("p (hd q) -> p hd q", q=TOK), "st")
        p.w(y, "st")
        if dbg.get("stop"):
            return ("V", p.cnt["V"])

    NB = lambda tt: 9 + tt
    kT_guard = 0
    vb_guard = 0
    pt_guard = 0
    pctx_guard = 0
    pT = xnT
    pp1_guard_v = p.cnt["V"]
    pp1_guard_s = 0
    ptrans_prev = 0
    m1_ld = ab["m1_ld"]
    kT_guard = p.cnt["T"]
    vb_guard = p.cnt["T"]
    ctx_done = {}
    for hd in range(16):
        p.w(y, "cc", cc_kv)
        p.w(y, "T", kT_guard)
        ii = y.dma_start_transpose(kTt[:, :], kv_out_d[:, hd * 128:(hd + 1) * 128])
        p.inc(ii, "yd", 16)
        kld = p.cnt["yd"]
        p.w(y, "T", vb_guard)
        vin = kv_out_d[:, 2048 + hd * 128: 2048 + (hd + 1) * 128].rearrange(
            "(kb q) e -> q kb e", q=128)
        vld = p.dma(y, vbt[:, :].rearrange("p (kb e) -> p kb e", e=128), vin, "yd")
        for tt in range(NTT):
            par = tt % 2
            nb = NB(tt)
            ext = 128 * nb
            p.w(t, "yd", kld)
            p.w(t, "S", pp1_guard_s)
            p.w(t, "V", pp1_guard_v)
            nks = (ext + 511) // 512
            for ks in range(nks):
                n = min(512, ext - 512 * ks)
                ii = t.matmul(pp1[:, 512 * ks:512 * ks + n],
                              qct[:, hd * TOK + tt * 128: hd * TOK + (tt + 1) * 128],
                              kTt[:, 512 * ks:512 * ks + n], start=True, stop=True)
            sc_c = p.inc(ii, "T")
            kT_guard = sc_c
            p.w(v, "yd", m1_ld)
            p.w(v, "T", sc_c)
            v.tensor_tensor(pp1[:, tt * 128:(tt + 1) * 128], pp1[:, tt * 128:(tt + 1) * 128],
                            tri[:, 0:128], ALU.add)
            v.tensor_tensor(pp1[:, (8 + tt) * 128:(9 + tt) * 128],
                            pp1[:, (8 + tt) * 128:(9 + tt) * 128], tri[:, 128:256], ALU.add)
            ii = v.tensor_tensor(pp1[:, 128 * (tt + 1):128 * (tt + 9)],
                                 pp1[:, 128 * (tt + 1):128 * (tt + 9)], m1, ALU.add)
            cN = p.inc(ii, "V"); p.w(v, "V", cN)
            ii = v.tensor_reduce(cols[:, 0:1], pp1[:, 0:ext], AX.X, ALU.max, negate=True)
            cN = p.inc(ii, "V"); p.w(v, "V", cN)
            ii = v.tensor_scalar_mul(cols[:, 1:2], cols[:, 0:1], ATTN_SCALE)
            cN = p.inc(ii, "V")
            p.w(s, "V", cN)
            p.w(s, "T", ptrans_prev)
            ii = s.activation(P_un[:, 0:ext], pp1[:, 0:ext], AF.Exp, bias=cols[:, 1:2],
                              scale=ATTN_SCALE, accum_out=cols[:, 2:3])
            ex_c = p.inc(ii, "S")
            pp1_guard_s = ex_c
            p.w(v, "S", ex_c)
            ii = v.reciprocal(cols[:, 3:4], cols[:, 2:3])
            cN = p.inc(ii, "V"); p.w(v, "V", cN)
            ii = v.tensor_scalar_mul(P_un[:, 0:ext], P_un[:, 0:ext], cols[:, 3:4])
            pn_c = p.inc(ii, "V")
            if dbg is not None and dbg.get("pun") is not None and hd == 0 and tt == 7:
                p.w(y, "V", pn_c)
                p.dma(y, dbg["pun"], P_un[:, 0:2048], "st")
                p.dma(y, dbg["cols"], cols[:, :], "st")
                p.w(v, "st")
                p.w(s, "st")
            slab_max = 12 if tt < 4 else 16
            p.w(t, "V", pn_c)
            for b8 in range(0, nb, 8):
                hi = min(nb, b8 + 8)
                p.w(t, "V", pt_guard)
                for kb in range(b8, hi):
                    ii = t.transpose(pt[:, (kb - b8) * 128:(kb - b8 + 1) * 128],
                                     P_un[:, kb * 128:(kb + 1) * 128], ident)
                tr_c = p.inc(ii, "T")
                p.w(v, "T", tr_c)
                oV = pT[:, :].rearrange("p (kb q) -> p kb q", q=TOK)[:, b8:hi,
                                                                    tt * 128:tt * 128 + 128]
                iV = pt[:, :].rearrange("p (k q) -> p k q", q=128)[:, 0:hi - b8, :]
                ii = v.tensor_copy(oV, iV)
                pt_guard = p.inc(ii, "V")
            for kb in range(nb, slab_max):
                ii = v.memset(pT[:, kb * TOK + tt * 128: kb * TOK + (tt + 1) * 128], 0.0)
                p.inc(ii, "V")
            pz_c = p.cnt["V"]
            ptrans_prev = p.cnt["T"]
        if dbg is not None and dbg.get("pt") is not None and hd == 0:
            p.w(y, "V", p.cnt["V"])
            p.dma(y, dbg["pt"].rearrange("(kb q2) q -> q2 kb q", q2=128),
                  pT[:, :].rearrange("p (kb q) -> p kb q", q=TOK), "st")
            p.w(y, "st")
            p.w(v, "st")
            p.w(t, "st")
        p.w(t, "yd", vld)
        p.w(t, "V", p.cnt["V"])
        for qs in range(2):
            kbmax = 12 if qs == 0 else 16
            p.w(t, "V", pctx_guard)
            for kb in range(kbmax):
                ii = t.matmul(pctx, vbt[:, kb * 128:(kb + 1) * 128],
                              pT[:, kb * TOK + qs * 512: kb * TOK + qs * 512 + 512],
                              start=(kb == 0), stop=(kb == kbmax - 1))
            cx_c = p.inc(ii, "T")
            vb_guard = cx_c
            p.w(v, "T", cx_c)
            ii = v.tensor_copy(qct[:, hd * TOK + qs * 512: hd * TOK + qs * 512 + 512], pctx)
            pctx_guard = p.inc(ii, "V")
        ctx_done[hd] = pctx_guard
    all_ctx = p.cnt["V"]
    if dbg is not None and dbg.get("ctx") is not None:
        p.w(y, "V", all_ctx)
        p.dma(y, dbg["ctx"].rearrange("(hd q2) q -> q2 hd q", q2=128),
              qct[:, :].rearrange("p (hd q) -> p hd q", q=TOK), "st")
        p.w(y, "st")
        if dbg.get("stop3"):
            return ("V", p.cnt["V"])

    wo_free = [p.cnt["T"], p.cnt["T"]]
    wo_loaded = [0, 0]

    def load_wo(ec, buf):
        p.w(y, "T", wo_free[buf])
        for r in range(8):
            rows = wdq_out_d[r * 1024 + 3 * 256: r * 1024 + 3 * 256 + 256,
                             ec * 128:(ec + 1) * 128]
            ii = y.dma_start_transpose(wsl[:, buf * 8192 + r * 256: buf * 8192 + (r + 1) * 256],
                                       rows)
            p.inc(ii, "yd", 16)
        wo_loaded[buf] = p.cnt["yd"]

    load_wo(0, 0)
    load_wo(1, 1)
    pout_guard = [p.cnt["V"], p.cnt["V"]]
    for ttg in range(4):
        ps2 = [pp1, ppo[:, 0:2048]]
        for ec in range(16):
            buf = ec % 2
            p.w(t, "yd", wo_loaded[buf])
            p.w(t, "V", all_ctx)
            for tloc in range(2):
                tt = 2 * ttg + tloc
                if ec == 0:
                    p.w(t, "V", pout_guard[tloc])
                for ms in range(4):
                    ii = t.matmul(ps2[tloc][:, ms * 512:(ms + 1) * 512],
                                  qct[:, ec * TOK + tt * 128: ec * TOK + (tt + 1) * 128],
                                  wsl[:, buf * 8192 + ms * 512:buf * 8192 + (ms + 1) * 512],
                                  start=(ec == 0), stop=(ec == 15))
            mmo_c = p.inc(ii, "T")
            if ttg == 3:
                wo_free[buf] = mmo_c
            if ttg == 0 and ec + 2 < 16:
                load_wo(ec + 2, buf)
        p.w(v, "T", p.cnt["T"])
        for tloc in range(2):
            tt = 2 * ttg + tloc
            for ms in range(4):
                xv = xs[:, tt * 2048 + ms * 512: tt * 2048 + (ms + 1) * 512]
                ii = v.scalar_tensor_tensor(xv, ps2[tloc][:, ms * 512:(ms + 1) * 512], 1.0,
                                            xv, ALU.mult, ALU.add)
            pout_guard[tloc] = p.inc(ii, "V")
    return ("V", p.cnt["V"])


def build_l0attn_launch(debug=0):
    nc = bass.Bass("TRN2", target_bir_lowering=False)
    x_in = nc.dram_tensor("x_in", [TOK, D], F32, kind="ExternalInput").ap()
    wsl_d = [nc.dram_tensor(f"w{i}_slice", [256, D], F32, kind="ExternalInput").ap()
             for i in range(4)]
    ident_d = nc.dram_tensor("ident_in", [128, 128], BF16, kind="ExternalInput").ap()
    cs_d = nc.dram_tensor("cs_in", [TOK, 128], F32, kind="ExternalInput").ap()
    tri_d = nc.dram_tensor("tri_in", [2, 128, 128], F32, kind="ExternalInput").ap()
    m1_d = nc.dram_tensor("m1_in", [128, 1024], F32, kind="ExternalInput").ap()
    x_out = nc.dram_tensor("x_out", [TOK, D], F32, kind="ExternalOutput").ap()
    dbg = None
    if debug:
        dbg = {"stop": debug == 2, "p2dump": debug <= 2}
        if debug <= 2:
            dbg.update({"wdq": nc.dram_tensor("dbg_wdq", [8192, 2048], BF16, kind="ExternalOutput").ap(),
                        "kv": nc.dram_tensor("dbg_kv", [2048, 4096], BF16, kind="ExternalOutput").ap(),
                        "qct": nc.dram_tensor("dbg_qct", [2048, 1024], BF16, kind="ExternalOutput").ap()})
        if debug >= 3:
            dbg["ctx"] = nc.dram_tensor("dbg_ctx", [2048, 1024], BF16, kind="ExternalOutput").ap()
            dbg["stop3"] = (debug >= 3)
        if debug >= 4:
            dbg["pun"] = nc.dram_tensor("dbg_pun", [128, 2048], BF16, kind="ExternalOutput").ap()
            dbg["cols"] = nc.dram_tensor("dbg_cols", [128, 16], F32, kind="ExternalOutput").ap()
            dbg["pt"] = nc.dram_tensor("dbg_pt", [2048, 1024], BF16, kind="ExternalOutput").ap()
    wdq_in_d = nc.dram_tensor("wdq_in", [1024, 2048], BF16).ap()
    wdq_out_d = nc.dram_tensor("wdq_out", [8192, 2048], BF16).ap()
    kv_in_d = nc.dram_tensor("kv_in", [1024, 4096], BF16).ap()
    kv_out_d = nc.dram_tensor("kv_out", [2048, 4096], BF16).ap()
    bar_in_d = nc.dram_tensor("bar_in", [16, 128], BF16).ap()
    bar_out2_d = nc.dram_tensor("bar_out2", [32, 128], BF16).ap()

    p = P(nc)
    p.sems("gd", "yd", "st", "T", "V", "S", "cc")
    xs = p.sb("xs", [128, NTT * 2048], F32)
    xnT = p.sb("xnT", [128, NIC * TOK], BF16)
    qct = p.sb("qct", [128, 16 * TOK], BF16)
    wsl = p.sb("wsl", [128, 2 * 8192], BF16)
    kTt = p.sb("kTt", [128, 2048], BF16)
    vbt = p.sb("vbt", [128, 2048], BF16)
    qro = p.sb("qro", [128, 1024], BF16)
    cs = p.sb("cs", [128, 8 * 128], F32)
    tri = p.sb("tri", [128, 256], F32)
    m1t = p.sb("m1t", [128, 1024], F32)
    ones1 = p.sb("ones1", [1, 128], BF16)
    cols = p.sb("cols", [128, 16], F32)
    dsm = p.sb("dsm", [128, 64], F32)
    ident = p.sb("identt", [128, 128], BF16)
    scratch = p.sb("scratch", [128, 3072], F32)
    ln = make_ln_tiles(p, scratch)
    pp1 = p.ps("pp1", [128, 2048], F32)
    ppo = p.ps("ppo", [128, 2048], F32)
    ptA = ppo[:, 0:512].bitcast(BF16)
    ptB = ppo[:, 512:1024].bitcast(BF16)
    qctf = qct[:, :].bitcast(F32)
    db = {"dqw": qctf[:, 0:4096], "dpos": qctf[:, 4096:6144], "dneg": qctf[:, 6144:8192],
          "dsm": dsm[:, :], "dqo": wsl[:, 0:2048]}
    ab = {"qct": qct, "wsl": wsl, "kTt": kTt, "vbt": vbt, "qro": qro, "cs": cs[:, :],
          "tri": tri[:, :], "m1": m1t[:, :], "cols": cols[:, :], "ones1": ones1[0:1, :],
          "ident": ident[:, :], "scratch": scratch, "pp1": pp1, "ppo": ppo,
          "wdq_out_d": wdq_out_d, "kv_in_d": kv_in_d, "kv_out_d": kv_out_d,
          "bar_in": bar_in_d, "bar_out2": bar_out2_d}

    with nc.Block() as block:
        @block.sync
        def _(sy):
            p.dma(p.y, ident[:, :], ident_d, "yd")
            p.dma(p.y, cs[:, :].rearrange("q2 (a j) -> q2 a j", j=128),
                  cs_d.rearrange("(a q2) j -> q2 a j", q2=128), "yd")
            p.dma(p.y, tri[:, 0:128], tri_d[0], "yd")
            p.dma(p.y, tri[:, 128:256], tri_d[1], "yd")
            ab["m1_ld"] = p.dma(p.y, m1t[:, :], m1_d, "yd")
            x_ready = []
            for tt in range(NTT):
                cval = p.dma(p.y, xs[:, tt * 2048:(tt + 1) * 2048],
                             x_in[tt * 128:(tt + 1) * 128, :], "yd")
                x_ready.append(("yd", cval))
            dq_yd = emit_dequant(p, db, wsl_d, wdq_in_d)
            p.w(p.g, "yd", dq_yd)
            icc = p.g.collective_compute("AllGather", ALU.bypass,
                                         replica_groups=[list(range(8))],
                                         ins=[wdq_in_d[:, :].opt()],
                                         outs=[wdq_out_d[:, :].opt()])
            cw = p.inc(icc, "cc")
            emit_ln2(p, xs, xnT, ln, ptA, ptB, ident[:, :], x_ready)
            ln_done = p.cnt["V"]
            xnT_ready = ("V", ln_done)
            emit_l0attn(p, xs, xnT, ab, xnT_ready, ln_done, p.cnt["V"], dbg=dbg, cc_wdq=cw)
            p.w(p.y, "V", p.cnt["V"])
            for tt in range(NTT):
                p.dma(p.y, x_out[tt * 128:(tt + 1) * 128, :],
                      xs[:, tt * 2048:(tt + 1) * 2048], "st")
            p.w(p.y, "st")
    p.close()
    return nc


def run_l0attn(x_full, inputs, debug=0):
    nc = build_l0attn_launch(debug=debug)
    ident = ident_bf16()
    xf = np.ascontiguousarray(x_full.reshape(8192, D))
    wq = np.asarray(inputs["L0_attn_w_q"], np.float32)
    wk = np.asarray(inputs["L0_attn_w_k"], np.float32)
    wv = np.asarray(inputs["L0_attn_w_v"], np.float32)
    wo = np.asarray(inputs["L0_attn_w_o"], np.float32)
    in_maps = []
    for c in range(N_CORES):
        cs_h, tri2, bm = host_prep_l0attn(inputs, c)
        m = {"x_in": np.ascontiguousarray(xf[c * TOK:(c + 1) * TOK]),
             "ident_in": ident, "cs_in": cs_h, "tri_in": tri2, "m1_in": bm}
        for i, w in enumerate([wq, wk, wv, wo]):
            m[f"w{i}_slice"] = np.ascontiguousarray(w[256 * c:256 * (c + 1)])
        in_maps.append(m)
    res = run_bass_kernel_spmd(nc, in_maps, core_ids=list(range(N_CORES)))
    out = np.concatenate([np.asarray(r["x_out"]) for r in res.results], axis=0)
    if debug:
        return out.reshape(x_full.shape), res
    return out.reshape(x_full.shape)



NFC = 11

def host_prep_l1moe(inputs):
    ws1T = np.ascontiguousarray(np.asarray(inputs["L1_ffn_ws1"], np.float32).T)
    ws3T = np.ascontiguousarray(np.asarray(inputs["L1_ffn_ws3"], np.float32).T)
    ws2T = np.ascontiguousarray(np.asarray(inputs["L1_ffn_ws2"], np.float32).T)
    gwT = np.ascontiguousarray(np.asarray(inputs["L1_ffn_gate_w"], np.float32).T)
    bias = np.asarray(inputs["L1_ffn_bias"], np.float32).reshape(8, 1)
    we1T = np.ascontiguousarray(np.asarray(inputs["L1_ffn_we1"], np.float32).transpose(0, 2, 1))
    we3T = np.ascontiguousarray(np.asarray(inputs["L1_ffn_we3"], np.float32).transpose(0, 2, 1))
    we2T = np.ascontiguousarray(np.asarray(inputs["L1_ffn_we2"], np.float32).transpose(0, 2, 1))
    return ws1T, ws3T, ws2T, gwT, bias, we1T, we3T, we2T


def emit_l1moe(p, xs, xnT, mb, xnT_ready):
    v, s, t, g, y = p.v, p.s, p.t, p.g, p.y
    wsb = mb["wsb"]; w2b = mb["w2b"]; act = mb["act"]; stmp = mb["stmp"]
    scT = mb["scT"]; routT = mb["routT"]; stok = mb["stok"]; rtok = mb["rtok"]
    ctok = mb["ctok"]; cols = mb["cols"]; biascol = mb["biascol"]; ident = mb["ident"]
    ph1 = mb["ph1"]; ph3 = mb["ph3"]; pdn = mb["pdn"]; ptr = pdn
    gw_d = mb["gw_d"]; ws1_d = mb["ws1_d"]; ws3_d = mb["ws3_d"]; ws2_d = mb["ws2_d"]
    we1_d = mb["we1_d"]; we3_d = mb["we3_d"]; we2_d = mb["we2_d"]
    sem_x, cnt_x = xnT_ready

    gwt = mb["gwt"]
    ld = p.dma(g, gwt[:, :].rearrange("q (ic e) -> q ic e", e=8),
               gw_d[:, :].rearrange("(ic q) e -> q ic e", q=128), "gd")
    p.w(t, "gd", ld)
    p.w(t, sem_x, cnt_x)
    for th in range(2):
        for ic in range(NIC):
            ii = t.matmul(ph1[0:8, th * 512:(th + 1) * 512], gwt[:, ic * 8:(ic + 1) * 8],
                          xnT[:, ic * TOK + th * 512: ic * TOK + th * 512 + 512],
                          start=(ic == 0), stop=(ic == NIC - 1))
    rt_c = p.inc(ii, "T")
    p.w(v, "S", p.cnt["S"])
    iz = v.memset(scT[:, :], 0.0)
    zc = p.inc(iz, "V")
    iz2 = v.memset(routT[:, :], 0.0)
    zc2 = p.inc(iz2, "V")
    p.w(s, "T", rt_c)
    p.w(s, "V", zc)
    ii = s.activation(scT[0:8, :], ph1[0:8, :], AF.Sigmoid)
    sg_c = p.inc(ii, "S")
    router_sg = sg_c
    p.w(v, "S", sg_c)
    ii = v.tensor_scalar_add(routT[0:8, :], scT[0:8, :], biascol[0:8, 0:1])
    ro_c = p.inc(ii, "V")
    p.w(v, "V", ro_c)
    identf = mb["identf"]
    p.w(v, "S", p.cnt["S"])
    iidc = v.tensor_copy(identf, ident)
    idc = p.inc(iidc, "V")
    ptr_guard = idc
    for tt in range(NTT):
        for which, (src, dst) in enumerate([(scT, stok), (routT, rtok)]):
            p.w(t, "V", ro_c)
            p.w(t, "V", ptr_guard)
            ii = t.transpose(ptr[:, 0:128], src[:, tt * 128:(tt + 1) * 128], identf)
            tr_c = p.inc(ii, "T")
            p.w(v, "T", tr_c)
            ii = v.tensor_copy(dst[:, tt * 8:(tt + 1) * 8], ptr[:, 0:8])
            ptr_guard = p.inc(ii, "V")
    for tt in range(NTT):
        sl = slice(tt * 8, (tt + 1) * 8)
        c0 = 0
        ii = v.tensor_reduce(cols[:, 0:1], rtok[:, sl], AX.X, ALU.max)
        cN = p.inc(ii, "V"); p.w(v, "V", cN)
        ii = v.tensor_scalar(stmp[:, 0:8], rtok[:, sl], cols[:, 0:1], None, ALU.is_ge)
        cN = p.inc(ii, "V"); p.w(v, "V", cN)
        ii = v.tensor_tensor(stmp[:, 8:16], stmp[:, 0:8], stok[:, sl], ALU.mult)
        cN = p.inc(ii, "V"); p.w(v, "V", cN)
        ii = v.tensor_reduce(cols[:, 1:2], stmp[:, 8:16], AX.X, ALU.add)
        cN = p.inc(ii, "V"); p.w(v, "V", cN)
        ii = v.scalar_tensor_tensor(stmp[:, 8:16], stmp[:, 0:8], -1e30, rtok[:, sl],
                                    ALU.mult, ALU.add)
        cN = p.inc(ii, "V"); p.w(v, "V", cN)
        ii = v.tensor_reduce(cols[:, 2:3], stmp[:, 8:16], AX.X, ALU.max)
        cN = p.inc(ii, "V"); p.w(v, "V", cN)
        ii = v.tensor_scalar(stmp[:, 16:24], stmp[:, 8:16], cols[:, 2:3], None, ALU.is_ge)
        cN = p.inc(ii, "V"); p.w(v, "V", cN)
        ii = v.tensor_tensor(stmp[:, 24:32], stmp[:, 16:24], stok[:, sl], ALU.mult)
        cN = p.inc(ii, "V"); p.w(v, "V", cN)
        ii = v.tensor_reduce(cols[:, 3:4], stmp[:, 24:32], AX.X, ALU.add)
        cN = p.inc(ii, "V"); p.w(v, "V", cN)
        ii = v.tensor_tensor(cols[:, 4:5], cols[:, 1:2], cols[:, 3:4], ALU.add)
        cN = p.inc(ii, "V"); p.w(v, "V", cN)
        ii = v.tensor_scalar_add(cols[:, 4:5], cols[:, 4:5], 1e-30)
        cN = p.inc(ii, "V"); p.w(v, "V", cN)
        ii = v.reciprocal(cols[:, 5:6], cols[:, 4:5])
        cN = p.inc(ii, "V"); p.w(v, "V", cN)
        ii = v.tensor_tensor(cols[:, 6:7], cols[:, 1:2], cols[:, 5:6], ALU.mult)
        p.inc(ii, "V")
        ii = v.tensor_tensor(cols[:, 7:8], cols[:, 3:4], cols[:, 5:6], ALU.mult)
        cN = p.inc(ii, "V"); p.w(v, "V", cN)
        ii = v.tensor_scalar(stmp[:, 0:8], stmp[:, 0:8], cols[:, 6:7], None, ALU.mult)
        cN = p.inc(ii, "V"); p.w(v, "V", cN)
        ii = v.scalar_tensor_tensor(stmp[:, 8:16], stmp[:, 16:24], cols[:, 7:8], stmp[:, 0:8],
                                    ALU.mult, ALU.add)
        cN = p.inc(ii, "V"); p.w(v, "V", cN)
        ii = v.tensor_scalar_mul(ctok[:, sl], stmp[:, 8:16], 0.5)
        p.inc(ii, "V")
    c_done = p.cnt["V"]
    if mb.get("dbg") is not None:
        p.w(y, "V", c_done)
        p.dma(y, mb["dbg"]["stok"], stok[:, :], "st")
        p.dma(y, mb["dbg"]["rtok"], rtok[:, :], "st")
        p.dma(y, mb["dbg"]["ctok"], ctok[:, :], "st")
        p.w(y, "st")
        p.w(v, "st")

    wsb_free = [0, 0]
    wsb_loaded = [0, 0]
    w2_loaded = 0
    w2_free = rt_c
    php_free_S = [router_sg, router_sg]
    php_free_V = [0, 0]
    pdn_free = [c_done, c_done]
    act_guard = c_done

    def load_h13(e, fc, b):
        s1d = ws1_d if e < 0 else we1_d[e]
        s3d = ws3_d if e < 0 else we3_d[e]
        p.w(g, "T", wsb_free[b])
        out1 = wsb[:, b * 4096:b * 4096 + 2048].rearrange("q (ic f) -> q ic f", f=128)
        p.dma(g, out1, s1d[:, fc * 128:(fc + 1) * 128].rearrange("(ic q) f -> q ic f", q=128), "gd")
        out3 = wsb[:, b * 4096 + 2048:b * 4096 + 4096].rearrange("q (ic f) -> q ic f", f=128)
        wsb_loaded[b] = p.dma(
            g, out3, s3d[:, fc * 128:(fc + 1) * 128].rearrange("(ic q) f -> q ic f", q=128), "gd")

    def load_w2(e):
        nonlocal w2_loaded
        s2d = ws2_d if e < 0 else we2_d[e]
        p.w(g, "T", w2_free)
        for fc in range(NFC):
            out = w2b[:, fc * 2048:(fc + 1) * 2048]
            w2_loaded = p.dma(g, out,
                              s2d[fc * 128:(fc + 1) * 128, :], "gd")

    seq = [-1] + list(range(8))
    loads = [(e, fc) for e in seq for fc in range(NFC)]
    for k in range(2):
        load_h13(loads[k][0], loads[k][1], k)
    load_w2(-1)
    li = 2
    for ei, e in enumerate(seq):
        for fc in range(NFC):
            b = (ei * NFC + fc) % 2
            p.w(t, "gd", wsb_loaded[b])
            p.w(t, sem_x, cnt_x)
            p.w(p.s, "V", act_guard)
            for th in range(2):
                par = th
                p.w(t, "S", php_free_S[par])
                p.w(t, "V", php_free_V[par])
                for ic in range(NIC):
                    rx = xnT[:, ic * TOK + th * 512: ic * TOK + th * 512 + 512]
                    t.matmul(ph1[:, th * 512:(th + 1) * 512],
                             wsb[:, b * 4096 + ic * 128:b * 4096 + ic * 128 + 128],
                             rx, start=(ic == 0), stop=(ic == NIC - 1))
                    ii = t.matmul(ph3[:, th * 512:(th + 1) * 512],
                                  wsb[:, b * 4096 + 2048 + ic * 128:b * 4096 + 2048 + ic * 128 + 128],
                                  rx, start=(ic == 0), stop=(ic == NIC - 1))
                mm_c = p.inc(ii, "T")
                if th == 1:
                    wsb_free[b] = mm_c
                    if li < len(loads):
                        load_h13(loads[li][0], loads[li][1], li % 2)
                        li += 1
                sv = stmp[:, th * 512:(th + 1) * 512]
                p.w(s, "T", mm_c)
                ii = s.activation(sv, ph1[:, th * 512:(th + 1) * 512], AF.Silu)
                si_c = p.inc(ii, "S")
                php_free_S[par] = si_c
                p.w(v, "S", si_c)
                ii = v.tensor_tensor(act[:, fc * 1024 + th * 512: fc * 1024 + th * 512 + 512],
                                     sv, ph3[:, th * 512:(th + 1) * 512], ALU.mult)
                php_free_V[par] = p.inc(ii, "V")
        acts_done = p.cnt["V"]
        p.w(t, "gd", w2_loaded)
        p.w(t, "V", acts_done)
        if e >= 0:
            p.w(t, "V", c_done)
        for tt in range(NTT):
            for sl4 in range(4):
                par = (4 * tt + sl4) % 2
                p.w(t, "V", pdn_free[par])
                pdv = pdn[:, par * 512:(par + 1) * 512]
                for fc in range(NFC):
                    ii = t.matmul(pdv, act[:, fc * 1024 + tt * 128: fc * 1024 + (tt + 1) * 128],
                                  w2b[:, fc * 2048 + sl4 * 512: fc * 2048 + (sl4 + 1) * 512],
                                  start=(fc == 0), stop=(fc == NFC - 1))
                dmm = p.inc(ii, "T")
                xv = xs[:, tt * 2048 + sl4 * 512: tt * 2048 + (sl4 + 1) * 512]
                p.w(v, "T", dmm)
                if e < 0:
                    ie = v.scalar_tensor_tensor(xv, pdv, 0.5, xv, ALU.mult, ALU.add)
                else:
                    ie = v.scalar_tensor_tensor(xv, pdv, ctok[:, tt * 8 + e: tt * 8 + e + 1],
                                                xv, ALU.mult, ALU.add)
                pdn_free[par] = p.inc(ie, "V")
        act_guard = p.cnt["V"]
        w2_free = p.cnt["T"]
        if ei + 1 < len(seq):
            load_w2(seq[ei + 1])
    return ("V", p.cnt["V"])


def make_l1moe_tiles(p, scratch, dr):
    mb = {
        "wsb": p.sb("wsb", [128, 2 * 4096], BF16),
        "w2b": p.sb("w2b", [128, NFC * 2048], BF16),
        "act": p.sb("actb", [128, NFC * 1024], BF16),
        "stmp": scratch[:, 0:1024],
        "scT": p.sb("scT", [128, 1024], F32),
        "routT": p.sb("routT", [128, 1024], F32),
        "stok": p.sb("stok", [128, 64], F32),
        "rtok": p.sb("rtok", [128, 64], F32),
        "ctok": p.sb("ctok", [128, 64], F32),
        "cols": p.sb("colsm", [128, 16], F32),
        "biascol": p.sb("biascol", [128, 1], F32),
        "gwt": p.sb("gwt", [128, 8 * 16], BF16),
        "ph1": p.ps("ph1", [128, 1024], F32),
        "ph3": p.ps("ph3", [128, 1024], F32),
        "pdn": p.ps("pdn", [128, 1024], F32),
    }
    mb.update(dr)
    return mb


def build_l1moe_launch(debug=0):
    nc = bass.Bass("TRN2", target_bir_lowering=False)
    x_in = nc.dram_tensor("x_in", [TOK, D], F32, kind="ExternalInput").ap()
    ws1_d = nc.dram_tensor("ws1T", [D, 1408], F32, kind="ExternalInput").ap()
    ws3_d = nc.dram_tensor("ws3T", [D, 1408], F32, kind="ExternalInput").ap()
    ws2_d = nc.dram_tensor("ws2T", [1408, D], F32, kind="ExternalInput").ap()
    gw_d = nc.dram_tensor("gwT", [D, 8], F32, kind="ExternalInput").ap()
    bias_d = nc.dram_tensor("biasv", [8, 1], F32, kind="ExternalInput").ap()
    we1_d = nc.dram_tensor("we1T", [8, D, 1408], F32, kind="ExternalInput").ap()
    we3_d = nc.dram_tensor("we3T", [8, D, 1408], F32, kind="ExternalInput").ap()
    we2_d = nc.dram_tensor("we2T", [8, 1408, D], F32, kind="ExternalInput").ap()
    ident_d = nc.dram_tensor("ident_in", [128, 128], BF16, kind="ExternalInput").ap()
    x_out = nc.dram_tensor("x_out", [TOK, D], F32, kind="ExternalOutput").ap()

    p = P(nc)
    p.sems("gd", "yd", "st", "T", "V", "S")
    xs = p.sb("xs", [128, NTT * 2048], F32)
    xnT = p.sb("xnT", [128, NIC * TOK], BF16)
    ident = p.sb("identt", [128, 128], BF16)
    scratch = p.sb("scratch", [128, 3072], F32)
    identf = scratch[:, 1024:1152]
    ln = make_ln_tiles(p, scratch)
    dr = {"gw_d": gw_d, "ws1_d": ws1_d, "ws3_d": ws3_d, "ws2_d": ws2_d,
          "we1_d": we1_d, "we3_d": we3_d, "we2_d": we2_d,
          "ident": ident[:, :], "identf": identf}
    mb = make_l1moe_tiles(p, scratch, dr)
    if debug:
        mb["dbg"] = {"stok": nc.dram_tensor("dbg_stok", [128, 64], F32, kind="ExternalOutput").ap(),
                     "rtok": nc.dram_tensor("dbg_rtok", [128, 64], F32, kind="ExternalOutput").ap(),
                     "ctok": nc.dram_tensor("dbg_ctok", [128, 64], F32, kind="ExternalOutput").ap()}
    ptA = p.ps("ptA", [128, 1024], BF16)
    ptB = p.ps("ptB", [128, 1024], BF16)

    with nc.Block() as block:
        @block.sync
        def _(sy):
            p.dma(p.y, ident[:, :], ident_d, "yd")
            p.dma(p.y, mb["biascol"][0:8, 0:1], bias_d, "yd")
            x_ready = []
            for tt in range(NTT):
                cval = p.dma(p.y, xs[:, tt * 2048:(tt + 1) * 2048],
                             x_in[tt * 128:(tt + 1) * 128, :], "yd")
                x_ready.append(("yd", cval))
            emit_ln2(p, xs, xnT, ln, ptA, ptB, ident[:, :], x_ready)
            xnT_ready = ("V", p.cnt["V"])
            emit_l1moe(p, xs, xnT, mb, xnT_ready)
            p.w(p.y, "V", p.cnt["V"])
            for tt in range(NTT):
                p.dma(p.y, x_out[tt * 128:(tt + 1) * 128, :],
                      xs[:, tt * 2048:(tt + 1) * 2048], "st")
            p.w(p.y, "st")
    p.close()
    return nc


def run_l1moe(x_full, inputs, debug=0):
    nc = build_l1moe_launch(debug=debug)
    ws1T, ws3T, ws2T, gwT, bias, we1T, we3T, we2T = host_prep_l1moe(inputs)
    xf = np.ascontiguousarray(x_full.reshape(8192, D))
    in_maps = []
    for c in range(N_CORES):
        in_maps.append({
            "x_in": np.ascontiguousarray(xf[c * TOK:(c + 1) * TOK]),
            "ws1T": ws1T, "ws3T": ws3T, "ws2T": ws2T, "gwT": gwT, "biasv": bias,
            "we1T": we1T, "we3T": we3T, "we2T": we2T, "ident_in": ident_bf16(),
        })
    res = run_bass_kernel_spmd(nc, in_maps, core_ids=list(range(N_CORES)))
    out = np.concatenate([np.asarray(r["x_out"]) for r in res.results], axis=0)
    if debug:
        return out.reshape(x_full.shape), res
    return out.reshape(x_full.shape)



def emit_l1attn(p, xs, xnT, ab, xnT_ready, ln_pt_done):
    v, s, t, g, y = p.v, p.s, p.t, p.g, p.y
    qct = ab["qct"]; wsl = ab["wsl"]; kTt = ab["kTt"]; vbt = ab["vbt"]
    qro = ab["qro"]; cs = ab["cs"]; cols = ab["cols"]; ones1 = ab["ones1"]
    ident = ab["ident"]; scratch = ab["scratch"]; pp1 = ab["pp1"]; ppo = ab["ppo"]
    wa_d = ab["wa_d"]; wo_d = ab["wo_d"]; brow = ab["brow"]
    kv_in_d = ab["kv_in_d"]; kv_out_d = ab["kv_out_d"]
    qtmp = scratch[:, 0:1024]
    scr2 = scratch[:, 2048:2560]
    P_un = scratch[:, 1024:2048].bitcast(BF16)
    pt = ppo[:, 1024:1536].bitcast(BF16)
    pctx = ppo[:, 1536:2048]
    pq = [ppo[:, 0:512], ppo[:, 512:1024]]
    ctxg = wsl

    im = v.memset(ones1, 1.0)
    p.inc(im, "V")
    qro_guard = {0: ("V", 0), 1: ("V", 0)}
    pt_evac = [0]

    slab_free = [0, 0]
    slab_loaded = [0, 0]

    def load_slab(sl, buf):
        p.w(g, "T", slab_free[buf])
        out = wsl[:, buf * 8192:buf * 8192 + 8192].rearrange("q (ic o) -> q ic o", o=512)
        in_ = wa_d[:, sl * 512:(sl + 1) * 512].rearrange("(ic q) o -> q ic o", q=128)
        slab_loaded[buf] = p.dma(g, out, in_, "gd")

    load_slab(0, 0)
    load_slab(1, 1)
    li = 2
    pq_free = [ln_pt_done, ln_pt_done]
    sem_x, cnt_x = xnT_ready
    for sl in range(9):
        buf = sl % 2
        p.w(t, "gd", slab_loaded[buf])
        p.w(t, sem_x, cnt_x)
        for tt in range(NTT):
            par = tt % 2
            p.w(t, "V", pq_free[par])
            psl = pq[par]
            for ic in range(NIC):
                ii = t.matmul(psl, xnT[:, ic * TOK + tt * 128: ic * TOK + (tt + 1) * 128],
                              wsl[:, buf * 8192 + ic * 512:buf * 8192 + (ic + 1) * 512],
                              start=(ic == 0), stop=(ic == NIC - 1))
            mm_c = p.inc(ii, "T")
            if tt == NTT - 1:
                slab_free[buf] = mm_c
                if li < 9:
                    load_slab(li, buf)
                    li += 1
            p.w(v, "T", mm_c)
            gsem, gcnt = qro_guard[tt % 2]
            if sl >= 5:
                vv = qro[:, (tt % 2) * 512:(tt % 2) * 512 + 512]
                p.w(v, gsem, gcnt)
                ii = v.tensor_copy(vv, psl)
                ev_c = p.inc(ii, "V")
                pq_free[par] = ev_c
                p.w(y, "V", ev_c)
                st_c = p.dma(y, kv_in_d[tt * 128:(tt + 1) * 128,
                                        512 + (sl - 5) * 512: 512 + (sl - 4) * 512], vv, "yd")
                qro_guard[tt % 2] = ("yd", st_c)
            else:
                qh = qtmp[:, par * 512:par * 512 + 512]
                ii = v.tensor_copy(qh, psl)
                ev_c = p.inc(ii, "V"); p.w(v, "V", ev_c)
                pq_free[par] = ev_c
                dst = qro[:, (tt % 2) * 512:(tt % 2) * 512 + 512]
                p.w(v, gsem, gcnt)
                rope_done = emit_rope(p, qh, dst, cs, tt, scr2)
                if sl <= 3:
                    p.w(t, "V", rope_done)
                    p.w(t, "V", pt_evac[0])
                    for h in range(4):
                        ii = t.transpose(pt[:, h * 128:(h + 1) * 128],
                                         dst[:, h * 128:(h + 1) * 128], ident)
                    tr_c = p.inc(ii, "T")
                    p.w(v, "T", tr_c)
                    oV = qct[:, :].rearrange("p (hd q) -> p hd q", q=TOK)[:, sl * 4:sl * 4 + 4,
                                                                         tt * 128:tt * 128 + 128]
                    iV = pt[:, :].rearrange("p (h q) -> p h q", q=128)[:, 0:4, :]
                    ii = v.tensor_copy(oV, iV)
                    pt_evac[0] = p.inc(ii, "V")
                    qro_guard[tt % 2] = ("V", pt_evac[0])
                else:
                    p.w(y, "V", rope_done)
                    st_c = p.dma(y, kv_in_d[tt * 128:(tt + 1) * 128, 0:512], dst, "yd")
                    qro_guard[tt % 2] = ("yd", st_c)
    p.w(g, "yd", p.cnt["yd"])
    icc = g.collective_compute("AllGather", ALU.bypass,
                               replica_groups=[[0, 1], [2, 3], [4, 5], [6, 7]],
                               ins=[kv_in_d[:, :].opt()], outs=[kv_out_d[:, :].opt()])
    p.inc(icc, "cc")
    icc2 = g.collective_compute("AllGather", ALU.bypass,
                                replica_groups=[[0, 1], [2, 3], [4, 5], [6, 7]],
                                ins=[ab["bar_in"][:, :].opt()], outs=[ab["bar_out2"][:, :].opt()])
    cc_kv = p.inc(icc2, "cc")

    pp1_guard_v = p.cnt["V"]
    kT_guard = p.cnt["T"]
    vb_guard = p.cnt["T"]
    pt_guard = p.cnt["V"]
    pctx_guard = p.cnt["V"]
    pout_guard = [p.cnt["V"], p.cnt["V"]]
    wo_free = [0, 0]
    wo_loaded = [0, 0]
    pT = xnT

    for grp in range(4):
        ctx_start_T = p.cnt["T"]
        for hd in range(16):
            kv = hd // 4
            if hd % 4 == 0:
                p.w(y, "cc", cc_kv)
                p.w(y, "T", kT_guard)
                ii = y.dma_start_transpose(kTt[:, :], kv_out_d[:, kv * 128:(kv + 1) * 128])
                p.inc(ii, "yd", 16)
                kld = p.cnt["yd"]
                p.w(y, "T", vb_guard)
                vin = kv_out_d[:, 512 + kv * 512: 512 + (kv + 1) * 512].rearrange(
                    "(kb q) e -> q kb e", q=128)
                vld = p.dma(y, vbt[:, :].rearrange("p (kb e) -> p kb e", e=512), vin, "yd")
            for tloc in range(2):
                tt = grp * 2 + tloc
                p.w(t, "yd", kld)
                p.w(t, "V", pp1_guard_v)
                for ks in range(4):
                    ii = t.matmul(pp1[:, 512 * ks:512 * ks + 512],
                                  qct[:, hd * TOK + tt * 128: hd * TOK + (tt + 1) * 128],
                                  kTt[:, 512 * ks:512 * ks + 512], start=True, stop=(ks == 3))
                sc_c = p.inc(ii, "T")
                kT_guard = sc_c
                p.w(v, "T", sc_c)
                ii = v.tensor_reduce(cols[:, 0:1], pp1[:, :], AX.X, ALU.max, negate=True)
                cN = p.inc(ii, "V"); p.w(v, "V", cN)
                ii = v.tensor_scalar_mul(cols[:, 1:2], cols[:, 0:1], ATTN_SCALE)
                cN = p.inc(ii, "V")
                p.w(s, "V", cN)
                ii = s.activation(P_un[:, :], pp1[:, :], AF.Exp, bias=cols[:, 1:2],
                                  scale=ATTN_SCALE, accum_out=cols[:, 2:3])
                ex_c = p.inc(ii, "S")
                p.w(v, "S", ex_c)
                ii = v.reciprocal(cols[:, 3:4], cols[:, 2:3])
                cN = p.inc(ii, "V"); p.w(v, "V", cN)
                ii = v.tensor_scalar_mul(P_un[:, :], P_un[:, :], cols[:, 3:4])
                pn_c = p.inc(ii, "V")
                pp1_guard_v = p.cnt["V"]
                p.w(t, "V", pn_c)
                for b8 in range(0, 16, 8):
                    p.w(t, "V", pt_guard)
                    for kb in range(b8, b8 + 8):
                        ii = t.transpose(pt[:, (kb - b8) * 128:(kb - b8 + 1) * 128],
                                         P_un[:, kb * 128:(kb + 1) * 128], ident)
                    tr_c = p.inc(ii, "T")
                    p.w(v, "T", tr_c)
                    oV = pT[:, :].rearrange("p (kb q) -> p kb q", q=256)[:, b8:b8 + 8,
                                                                        tloc * 128:tloc * 128 + 128]
                    iV = pt[:, :].rearrange("p (k q) -> p k q", q=128)[:, 0:8, :]
                    ii = v.tensor_copy(oV, iV)
                    pt_guard = p.inc(ii, "V")
            p.w(t, "yd", vld)
            p.w(t, "V", pt_guard)
            for eb in range(4):
                p.w(t, "V", pctx_guard)
                for kb in range(16):
                    ii = t.matmul(pctx[:, 0:256], vbt[:, kb * 512 + eb * 128: kb * 512 + (eb + 1) * 128],
                                  pT[:, kb * 256:kb * 256 + 256], start=(kb == 0), stop=(kb == 15))
                cx_c = p.inc(ii, "T")
                vb_guard = cx_c
                p.w(v, "T", cx_c)
                ec = hd * 4 + eb
                ii = v.tensor_copy(ctxg[:, ec * 256:(ec + 1) * 256], pctx[:, 0:256])
                pctx_guard = p.inc(ii, "V")
        all_ctx = p.cnt["V"]

        def load_wo(ec, buf):
            p.w(g, "T", wo_free[buf])
            ii = g.dma_start(out=vbt[:, buf * 2048:(buf + 1) * 2048],
                             in_=wo_d[ec * 128:(ec + 1) * 128, :])
            p.inc(ii, "gd", 16)
            wo_loaded[buf] = p.cnt["gd"]

        wo_free = [p.cnt["T"], p.cnt["T"]]
        load_wo(0, 0)
        load_wo(1, 1)
        ps2 = [pp1, ppo[:, 0:2048]]
        for ec in range(64):
            buf = ec % 2
            p.w(t, "gd", wo_loaded[buf])
            p.w(t, "V", all_ctx)
            for tloc in range(2):
                if ec == 0:
                    p.w(t, "V", pout_guard[tloc])
                for ms in range(4):
                    ii = t.matmul(ps2[tloc][:, ms * 512:(ms + 1) * 512],
                                  ctxg[:, ec * 256 + tloc * 128: ec * 256 + tloc * 128 + 128],
                                  vbt[:, buf * 2048 + ms * 512: buf * 2048 + (ms + 1) * 512],
                                  start=(ec == 0), stop=False)
            mmo_c = p.inc(ii, "T")
            wo_free[buf] = mmo_c
            if ec + 2 < 64:
                load_wo(ec + 2, buf)
        p.w(t, "gd", ab["brow_ld"])
        for tloc in range(2):
            for ms in range(4):
                ii = t.matmul(ps2[tloc][:, ms * 512:(ms + 1) * 512], ones1,
                              brow[0:1, ms * 512:(ms + 1) * 512], start=False,
                              stop=True)
        mmo_c = p.inc(ii, "T")
        p.w(v, "T", mmo_c)
        for tloc in range(2):
            tt = grp * 2 + tloc
            for ms in range(4):
                xv = xs[:, tt * 2048 + ms * 512: tt * 2048 + (ms + 1) * 512]
                ii = v.scalar_tensor_tensor(xv, ps2[tloc][:, ms * 512:(ms + 1) * 512], 1.0,
                                            xv, ALU.mult, ALU.add)
            pout_guard[tloc] = p.inc(ii, "V")
        vb_guard = p.cnt["T"]
        pt_guard = p.cnt["V"]
        pctx_guard = p.cnt["V"]
        pp1_guard_v = p.cnt["V"]
    return ("V", p.cnt["V"])


def build_l1attn_launch():
    nc = bass.Bass("TRN2", target_bir_lowering=False)
    x_in = nc.dram_tensor("x_in", [TOK, D], F32, kind="ExternalInput").ap()
    wa_d = nc.dram_tensor("waT", [D, 4608], F32, kind="ExternalInput").ap()
    wo_d = nc.dram_tensor("woT", [8192, D], F32, kind="ExternalInput").ap()
    bo_d = nc.dram_tensor("bout", [1, D], F32, kind="ExternalInput").ap()
    ident_d = nc.dram_tensor("ident_in", [128, 128], BF16, kind="ExternalInput").ap()
    cs_d = nc.dram_tensor("cs_in", [TOK, 128], F32, kind="ExternalInput").ap()
    x_out = nc.dram_tensor("x_out", [TOK, D], F32, kind="ExternalOutput").ap()
    kv_in_d = nc.dram_tensor("kv_in", [1024, 2560], BF16).ap()
    kv_out_d = nc.dram_tensor("kv_out", [2048, 2560], BF16).ap()
    bar_in_d = nc.dram_tensor("bar_in", [16, 128], BF16).ap()
    bar_out2_d = nc.dram_tensor("bar_out2", [32, 128], BF16).ap()

    p = P(nc)
    p.sems("gd", "yd", "st", "T", "V", "S", "cc")
    xs = p.sb("xs", [128, NTT * 2048], F32)
    xnT = p.sb("xnT", [128, NIC * TOK], BF16)
    qct = p.sb("qct", [128, 16 * TOK], BF16)
    wsl = p.sb("wsl", [128, 2 * 8192], BF16)
    kTt = p.sb("kTt", [128, 2048], BF16)
    vbt = p.sb("vbt", [128, 16 * 512], BF16)
    qro = p.sb("qro", [128, 1024], BF16)
    cs = p.sb("cs", [128, 8 * 128], F32)
    brow = p.sb("brow", [1, 2048], BF16)
    ones1 = p.sb("ones1", [1, 128], BF16)
    cols = p.sb("cols", [128, 16], F32)
    ident = p.sb("identt", [128, 128], BF16)
    scratch = p.sb("scratch", [128, 3072], F32)
    ln = make_ln_tiles(p, scratch, nbuf=1)
    pp1 = p.ps("pp1", [128, 2048], F32)
    ppo = p.ps("ppo", [128, 2048], F32)
    ptA = ppo[:, 0:512].bitcast(BF16)
    ptB = ppo[:, 512:1024].bitcast(BF16)
    ab = {"qct": qct, "wsl": wsl, "kTt": kTt, "vbt": vbt, "qro": qro, "cs": cs[:, :],
          "cols": cols[:, :], "ones1": ones1[0:1, :], "ident": ident[:, :],
          "scratch": scratch, "pp1": pp1, "ppo": ppo, "wa_d": wa_d, "wo_d": wo_d,
          "brow": brow, "kv_in_d": kv_in_d, "kv_out_d": kv_out_d}

    with nc.Block() as block:
        @block.sync
        def _(sy):
            p.dma(p.y, ident[:, :], ident_d, "yd")
            p.dma(p.y, cs[:, :].rearrange("q2 (a j) -> q2 a j", j=128),
                  cs_d.rearrange("(a q2) j -> q2 a j", q2=128), "yd")
            ab["brow_ld"] = p.dma(p.g, brow[0:1, :], bo_d, "gd")
            x_ready = []
            for tt in range(NTT):
                cval = p.dma(p.y, xs[:, tt * 2048:(tt + 1) * 2048],
                             x_in[tt * 128:(tt + 1) * 128, :], "yd")
                x_ready.append(("yd", cval))
            emit_ln2(p, xs, xnT, ln, ptA, ptB, ident[:, :], x_ready, nbuf=1)
            ln_done = p.cnt["V"]
            emit_l1attn(p, xs, xnT, ab, ("V", ln_done), ln_done)
            p.w(p.y, "V", p.cnt["V"])
            for tt in range(NTT):
                p.dma(p.y, x_out[tt * 128:(tt + 1) * 128, :],
                      xs[:, tt * 2048:(tt + 1) * 2048], "st")
            p.w(p.y, "st")
    p.close()
    return nc


def run_l1attn(x_full, inputs):
    nc = build_l1attn_launch()
    wa = np.ascontiguousarray(np.asarray(inputs["L1_attn_W_attn"], np.float32))
    wo = np.ascontiguousarray(np.asarray(inputs["L1_attn_W_out"], np.float32))
    bo = np.ascontiguousarray(np.asarray(inputs["L1_attn_b_out"], np.float32).reshape(1, D))
    xf = np.ascontiguousarray(x_full.reshape(8192, D))
    in_maps = []
    for c in range(N_CORES):
        cs_h, _, _ = host_prep_l0attn(inputs, c)
        in_maps.append({"x_in": np.ascontiguousarray(xf[c * TOK:(c + 1) * TOK]),
                        "waT": wa, "woT": wo, "bout": bo, "ident_in": ident_bf16(),
                        "cs_in": cs_h})
    res = run_bass_kernel_spmd(nc, in_maps, core_ids=list(range(N_CORES)))
    out = np.concatenate([np.asarray(r["x_out"]) for r in res.results], axis=0)
    return out.reshape(x_full.shape)



def _barrier(p):
    engs = [p.t, p.v, p.s, p.g, p.y]
    sems = ["T", "V", "S", "gd", "yd", "st"]
    for e in engs:
        for sm in sems:
            p.w(e, sm)


def build_fused():
    nc = bass.Bass("TRN2", target_bir_lowering=False)
    x_in = nc.dram_tensor("x_in", [TOK, D], F32, kind="ExternalInput").ap()
    x_out = nc.dram_tensor("x_out", [TOK, D], F32, kind="ExternalOutput").ap()
    wg_d = nc.dram_tensor("wg", [D, FF], F32, kind="ExternalInput").ap()
    wu_d = nc.dram_tensor("wu", [D, FF], F32, kind="ExternalInput").ap()
    wd_d = nc.dram_tensor("wd", [FF, D], F32, kind="ExternalInput").ap()
    wsl_d = [nc.dram_tensor(f"w{i}_slice", [256, D], F32, kind="ExternalInput").ap()
             for i in range(4)]
    cs_d = nc.dram_tensor("cs_in", [TOK, 128], F32, kind="ExternalInput").ap()
    tri_d = nc.dram_tensor("tri_in", [2, 128, 128], F32, kind="ExternalInput").ap()
    m1_d = nc.dram_tensor("m1_in", [128, 1024], F32, kind="ExternalInput").ap()
    ident_d = nc.dram_tensor("ident_in", [128, 128], BF16, kind="ExternalInput").ap()
    ws1_d = nc.dram_tensor("ws1T", [D, 1408], F32, kind="ExternalInput").ap()
    ws3_d = nc.dram_tensor("ws3T", [D, 1408], F32, kind="ExternalInput").ap()
    ws2_d = nc.dram_tensor("ws2T", [1408, D], F32, kind="ExternalInput").ap()
    gw_d = nc.dram_tensor("gwT", [D, 8], F32, kind="ExternalInput").ap()
    bias_d = nc.dram_tensor("biasv", [8, 1], F32, kind="ExternalInput").ap()
    we1_d = nc.dram_tensor("we1T", [8, D, 1408], F32, kind="ExternalInput").ap()
    we3_d = nc.dram_tensor("we3T", [8, D, 1408], F32, kind="ExternalInput").ap()
    we2_d = nc.dram_tensor("we2T", [8, 1408, D], F32, kind="ExternalInput").ap()
    wa_d = nc.dram_tensor("waT", [D, 4608], F32, kind="ExternalInput").ap()
    wo_d = nc.dram_tensor("woT", [8192, D], F32, kind="ExternalInput").ap()
    bo_d = nc.dram_tensor("bout", [1, D], F32, kind="ExternalInput").ap()
    wdq_in_d = nc.dram_tensor("wdq_in", [1024, 2048], BF16).ap()
    wdq_out_d = nc.dram_tensor("wdq_out", [8192, 2048], BF16).ap()
    kv_in_d = nc.dram_tensor("kv_in", [1024, 4096], BF16).ap()
    kv_out_d = nc.dram_tensor("kv_out", [2048, 4096], BF16).ap()
    kv1_in_d = nc.dram_tensor("kv1_in", [1024, 2560], BF16).ap()
    kv1_out_d = nc.dram_tensor("kv1_out", [2048, 2560], BF16).ap()
    bar_in_d = nc.dram_tensor("bar_in", [16, 128], BF16).ap()
    bar_out8_d = nc.dram_tensor("bar_out8", [128, 128], BF16).ap()
    bar_out2_d = nc.dram_tensor("bar_out2", [32, 128], BF16).ap()

    p = P(nc)
    p.sems("gd", "yd", "sd", "st", "T", "V", "S", "cc")
    xs = p.sb("xs", [128, NTT * 2048], F32)
    xnT = p.sb("xnT", [128, NIC * TOK], BF16)
    arena = p.sb("arena", [128, 44032], BF16)
    scratch = p.sb("scratch", [128, 3072], F32)
    ident = p.sb("identt", [128, 128], BF16)
    cs = arena[:, 41984:44032].bitcast(F32)
    tri = p.sb("tri", [128, 256], F32)
    m1t = p.sb("m1t", [128, 1024], F32)
    brow = p.sb("brow", [1, 2048], BF16)
    ones1 = p.sb("ones1", [1, 128], BF16)
    cols = p.sb("cols", [128, 16], F32)
    biascol = p.sb("biascol", [128, 1], F32)
    dsm = scratch[:, 2496:2560]
    stok = scratch[:, 2560:2624]
    rtok = scratch[:, 2624:2688]
    ctok = scratch[:, 2688:2752]
    ln = make_ln_tiles(p, scratch, nbuf=1)
    pp1 = p.ps("pp1", [128, 2048], F32)
    ppo = p.ps("ppo", [128, 2048], F32)
    ptA = ppo[:, 0:512].bitcast(BF16)
    ptB = ppo[:, 512:1024].bitcast(BF16)
    arf = arena[:, :].bitcast(F32)

    db = {"dqw": arf[:, 0:4096], "dpos": arf[:, 4096:6144], "dneg": arf[:, 6144:8192],
          "dsm": dsm[:, :], "dqo": arena[:, 16384:18432]}
    fb = {"wgub": arena[:, 0:8192], "wdb": arena[:, 8192:24576],
          "hf": arf[:, 12288:16384], "habs": arf[:, 16384:18432],
          "hsb": arena[:, 36864:40960], "tsc": scratch[:, 0:3072],
          "pg": pp1[:, 0:1024], "pu": pp1[:, 1024:2048], "pd": ppo[:, 1024:2048],
          "wg_d": wg_d, "wu_d": wu_d, "wd_d": wd_d}
    ab0 = {"qct": arena[:, 0:16384], "wsl": arena[:, 16384:32768],
           "kTt": arena[:, 32768:34816], "vbt": arena[:, 34816:36864],
           "qro": arena[:, 36864:37888], "cs": cs, "tri": tri[:, :],
           "m1": m1t[:, :], "cols": cols[:, :], "ones1": ones1[0:1, :],
           "ident": ident[:, :], "scratch": scratch, "pp1": pp1, "ppo": ppo,
           "wdq_out_d": wdq_out_d, "kv_in_d": kv_in_d, "kv_out_d": kv_out_d,
           "bar_in": bar_in_d, "bar_out2": bar_out2_d}
    mb = {"wsb": arena[:, 0:8192], "w2b": arena[:, 8192:30720],
          "act": arena[:, 30720:41984],
          "stmp": scratch[:, 0:1024],
          "scT": scratch[:, 1152:2176], "routT": scratch[:, 0:1024],
          "stok": stok, "rtok": rtok, "ctok": ctok,
          "cols": cols[:, :], "biascol": biascol, "gwt": arena[:, 8192:8320],
          "ph1": pp1[:, 0:1024], "ph3": pp1[:, 1024:2048], "pdn": ppo[:, 1024:2048],
          "gw_d": gw_d, "ws1_d": ws1_d, "ws3_d": ws3_d, "ws2_d": ws2_d,
          "we1_d": we1_d, "we3_d": we3_d, "we2_d": we2_d,
          "ident": ident[:, :], "identf": scratch[:, 2176:2304]}
    ab1 = {"qct": arena[:, 0:16384], "wsl": arena[:, 16384:32768],
           "vbt": arena[:, 32768:40960], "kTt": arena[:, 40960:43008],
           "qro": arena[:, 36864:37888], "cs": cs, "cols": cols[:, :],
           "ones1": ones1[0:1, :], "ident": ident[:, :], "scratch": scratch,
           "pp1": pp1, "ppo": ppo, "wa_d": wa_d, "wo_d": wo_d, "brow": brow,
           "kv_in_d": kv1_in_d, "kv_out_d": kv1_out_d,
           "bar_in": bar_in_d, "bar_out2": bar_out2_d}

    with nc.Block() as block:
        @block.sync
        def _(sy):
            iz = p.v.memset(arena[:, :], 0.0)
            p.inc(iz, "V")
            iz = p.v.memset(xnT[:, :], 0.0)
            p.inc(iz, "V")
            iz = p.v.memset(scratch[:, :], 0.0)
            p.inc(iz, "V")
            iz = p.v.memset(cols[:, :], 0.0)
            zi_c = p.inc(iz, "V")
            p.w(p.y, "V", zi_c)
            p.w(p.g, "V", zi_c)
            p.dma(p.y, ident[:, :], ident_d, "yd")
            p.dma(p.y, cs.rearrange("q2 (a j) -> q2 a j", j=128),
                  cs_d.rearrange("(a q2) j -> q2 a j", q2=128), "yd")
            p.dma(p.y, tri[:, 0:128], tri_d[0], "yd")
            p.dma(p.y, tri[:, 128:256], tri_d[1], "yd")
            ab0["m1_ld"] = p.dma(p.y, m1t[:, :], m1_d, "yd")
            p.dma(p.y, biascol[0:8, 0:1], bias_d, "yd")
            ab1["brow_ld"] = p.dma(p.g, brow[0:1, :], bo_d, "gd")
            x_ready = []
            for tt in range(NTT):
                cval = p.dma(p.y, xs[:, tt * 2048:(tt + 1) * 2048],
                             x_in[tt * 128:(tt + 1) * 128, :], "yd")
                x_ready.append(("yd", cval))
            dq_yd = emit_dequant(p, db, wsl_d, wdq_in_d)
            p.w(p.g, "yd", dq_yd)
            icc = p.g.collective_compute("AllGather", ALU.bypass,
                                         replica_groups=[list(range(8))],
                                         ins=[wdq_in_d[:, :].opt()],
                                         outs=[wdq_out_d[:, :].opt()])
            p.inc(icc, "cc")
            icc2 = p.g.collective_compute("AllGather", ALU.bypass,
                                          replica_groups=[list(range(8))],
                                          ins=[bar_in_d[:, :].opt()],
                                          outs=[bar_out8_d[:, :].opt()])
            cc_wdq = p.inc(icc2, "cc")
            _barrier(p)
            emit_ln2(p, xs, xnT, ln, ptA, ptB, ident[:, :], x_ready, nbuf=1)
            emit_l0ffn(p, xs, xnT, fb, ("V", p.cnt["V"]), coef=0.5)
            _barrier(p)
            res = [("yd", 0)] * NTT
            emit_ln2(p, xs, xnT, ln, ptA, ptB, ident[:, :], [("V", 0)] * NTT, nbuf=1)
            emit_l0attn(p, xs, xnT, ab0, ("V", p.cnt["V"]), p.cnt["V"], p.cnt["V"],
                        cc_wdq=cc_wdq)
            _barrier(p)
            emit_ln2(p, xs, xnT, ln, ptA, ptB, ident[:, :], [("V", 0)] * NTT, nbuf=1)
            emit_l0ffn(p, xs, xnT, fb, ("V", p.cnt["V"]), coef=0.5)
            _barrier(p)
            emit_ln2(p, xs, xnT, ln, ptA, ptB, ident[:, :], [("V", 0)] * NTT, nbuf=1)
            emit_l1moe(p, xs, xnT, mb, ("V", p.cnt["V"]))
            _barrier(p)
            emit_ln2(p, xs, xnT, ln, ptA, ptB, ident[:, :], [("V", 0)] * NTT, nbuf=1)
            emit_l1attn(p, xs, xnT, ab1, ("V", p.cnt["V"]), p.cnt["V"])
            _barrier(p)
            emit_ln2(p, xs, xnT, ln, ptA, ptB, ident[:, :], [("V", 0)] * NTT, nbuf=1)
            emit_l1moe(p, xs, xnT, mb, ("V", p.cnt["V"]))
            p.w(p.y, "V", p.cnt["V"])
            for tt in range(NTT):
                p.dma(p.y, x_out[tt * 128:(tt + 1) * 128, :],
                      xs[:, tt * 2048:(tt + 1) * 2048], "st")
            p.w(p.y, "st")
    p.close()
    return nc


def run_fused(inputs):
    x = np.ascontiguousarray(np.asarray(inputs["L0_attn_x"], np.float32))
    shape = x.shape
    wg, wu, wd = host_prep_l0ffn(inputs)
    ws1T, ws3T, ws2T, gwT, bias, we1T, we3T, we2T = host_prep_l1moe(inputs)
    wa = np.ascontiguousarray(np.asarray(inputs["L1_attn_W_attn"], np.float32))
    wo = np.ascontiguousarray(np.asarray(inputs["L1_attn_W_out"], np.float32))
    bo = np.ascontiguousarray(np.asarray(inputs["L1_attn_b_out"], np.float32).reshape(1, D))
    wq = np.asarray(inputs["L0_attn_w_q"], np.float32)
    wk = np.asarray(inputs["L0_attn_w_k"], np.float32)
    wv = np.asarray(inputs["L0_attn_w_v"], np.float32)
    wol = np.asarray(inputs["L0_attn_w_o"], np.float32)
    ident = ident_bf16()
    xf = np.ascontiguousarray(x.reshape(8192, D))
    nc = build_fused()
    in_maps = []
    for c in range(N_CORES):
        cs_h, tri2, bm = host_prep_l0attn(inputs, c)
        m = {"x_in": np.ascontiguousarray(xf[c * TOK:(c + 1) * TOK]),
             "wg": wg, "wu": wu, "wd": wd,
             "cs_in": cs_h, "tri_in": tri2, "m1_in": bm, "ident_in": ident,
             "ws1T": ws1T, "ws3T": ws3T, "ws2T": ws2T, "gwT": gwT, "biasv": bias,
             "we1T": we1T, "we3T": we3T, "we2T": we2T,
             "waT": wa, "woT": wo, "bout": bo}
        for i, w in enumerate([wq, wk, wv, wol]):
            m[f"w{i}_slice"] = np.ascontiguousarray(w[256 * c:256 * (c + 1)])
        in_maps.append(m)
    res = run_bass_kernel_spmd(nc, in_maps, core_ids=list(range(N_CORES)))
    out = np.concatenate([np.asarray(r["x_out"]) for r in res.results], axis=0)
    return np.ascontiguousarray(out.reshape(shape)).astype(np.float32)



def kernel(**inputs):
    return run_fused(inputs)
